# version 05 (final)
# baseline (speedup 1.0000x reference)
import numpy as np
import concourse.bass as bass
import concourse.mybir as mybir
from concourse.bass_utils import run_bass_kernel_spmd
from contextlib import ExitStack

F32 = mybir.dt.float32
AF = mybir.ActivationFunctionType
AX = mybir.AxisListType.X
ALU = mybir.AluOpType

S, D, HD, NKV, NQH, R = 2048, 4096, 128, 8, 32, 32
NIC = D // 128
STRIP = 512
NSTR = S // STRIP
ISQHD = float(1.0 / np.sqrt(128.0))
EPS = 1e-5
NEG = -1.0e30


class Sem:
    def __init__(self, h, step):
        self.h = h
        self.step = step
        self.n = 0


class Prog:
    def __init__(self, eng_sems):
        self.ops = []
        self.eng_sems = eng_sems

    def op(self, eng, fn, waits=(), drain=False, inc=None):
        sem = inc if inc is not None else self.eng_sems.get(eng)
        for (s, v) in waits:
            assert v <= s.n, f"wait on future value {v} > {s.n} ({eng})"
        self.ops.append((eng, fn, list(waits), drain, sem))
        sem.n += sem.step
        return sem.n

    def finalize(self):
        owner = {id(s): n for n, s in self.eng_sems.items()}
        needed = {n: set() for n in self.eng_sems}
        for (ename, fn, waits, drain, sem) in self.ops:
            for (s, v) in waits:
                o = owner.get(id(s))
                if o is not None and o != ename:
                    needed[o].add(v)
        self._needed = needed

    def emit_engine(self, name, eng):
        needed = self._needed.get(name, set())
        waited = {}
        count = 0
        for (ename, fn, waits, drain, sem) in self.ops:
            if ename != name:
                continue
            if sem.step == 1:
                count += 1
            for (s, v) in waits:
                if waited.get(id(s), -1) < v:
                    eng.wait_ge(s.h, v)
                    waited[id(s)] = v
            instr = fn(eng)
            if drain or (sem.step == 1 and count in needed):
                instr = eng.drain()
            instr.then_inc(sem.h, sem.step)


def build_nc():
    nc = bass.Bass("TRN2", target_bir_lowering=False)
    x = nc.dram_tensor("x", [S, D], F32, kind="ExternalInput").ap()
    wq = nc.dram_tensor("wq", [D, D], F32, kind="ExternalInput").ap()
    wk = nc.dram_tensor("wk", [D, NKV * HD], F32, kind="ExternalInput").ap()
    wv = nc.dram_tensor("wv", [D, NKV * HD], F32, kind="ExternalInput").ap()
    wo = nc.dram_tensor("wo", [D, D], F32, kind="ExternalInput").ap()
    wt = nc.dram_tensor("wt", [D, D], F32, kind="ExternalInput").ap()
    at = nc.dram_tensor("at", [D, R], F32, kind="ExternalInput").ap()
    bt = nc.dram_tensor("bt", [R, D], F32, kind="ExternalInput").ap()
    magr = nc.dram_tensor("magr", [1, D], F32, kind="ExternalInput").ap()
    cosf = nc.dram_tensor("cosf", [128, S], F32, kind="ExternalInput").ap()
    sinf = nc.dram_tensor("sinf", [128, S], F32, kind="ExternalInput").ap()
    maskc = nc.dram_tensor("maskc", [128, 4 * STRIP], F32, kind="ExternalInput").ap()
    ident = nc.dram_tensor("ident", [128, 128], F32, kind="ExternalInput").ap()
    onesr = nc.dram_tensor("onesr", [1, 128], F32, kind="ExternalInput").ap()
    onesc = nc.dram_tensor("onesc", [128, 1], F32, kind="ExternalInput").ap()
    y = nc.dram_tensor("y", [S, D], F32, kind="ExternalOutput").ap()
    xnf = nc.dram_tensor("xnf", [D, S], F32, kind="Internal").ap()
    qf = nc.dram_tensor("qf", [D, S], F32, kind="Internal").ap()
    kf = nc.dram_tensor("kf", [NKV * HD, S], F32, kind="Internal").ap()
    vf = nc.dram_tensor("vf", [NKV * HD, S], F32, kind="Internal").ap()
    of = nc.dram_tensor("of", [D, S], F32, kind="Internal").ap()

    with ExitStack() as ctx:
        sbt = lambda n, sh: ctx.enter_context(nc.sbuf_tensor(n, sh, F32))
        pst = lambda n, sh: ctx.enter_context(nc.psum_tensor(n, sh, F32))
        big1 = sbt("big1", [128, 16384])
        big2 = sbt("big2", [128, 16384])
        rowp = sbt("rowp", [128, 4096])
        at_s = sbt("at_s", [128, NIC * R])
        id_s = sbt("id_s", [128, 128])
        on1_s = sbt("on1_s", [1, 128])
        onc_s = sbt("onc_s", [128, 1])
        lnb = sbt("lnb", [128, 64])
        mrow = sbt("mrow", [1, 4736])
        psA = pst("psA", [128, 2048])
        psB = pst("psB", [128, 2048])

        def mks(name, step):
            return Sem(ctx.enter_context(nc.semaphore(name)), step)

        TT = mks("tt", 1)
        VV = mks("vv", 1)
        SS = mks("ss", 1)
        MI = mks("mi", 16)
        XNS = mks("xns", 16)
        KVS = mks("kvs", 16)
        WP = [mks("wpa", 16), mks("wpb", 16)]
        SA = [mks("saa", 16), mks("sab", 16)]
        XT = [mks("xta", 16), mks("xtb", 16)]
        STX = mks("stx", 16)
        STK = [mks("stka", 16), mks("stkb", 16)]
        STO = mks("sto", 16)
        STY = mks("sty", 16)
        ALL_SEMS = [TT, VV, SS, MI, XNS, KVS, WP[0], WP[1], SA[0], SA[1],
                    XT[0], XT[1], STX, STK[0], STK[1], STO, STY]
        block = ctx.enter_context(nc.Block())

        P = Prog({"tensor": TT, "vector": VV, "scalar": SS})

        def LD(q, dst, src, waits=()):
            return P.op("sync", lambda e, d=dst, s=src: e.dma_start(out=d, in_=s),
                        waits, inc=q)

        def ST(q, dst, src, waits=()):
            return P.op("gpsimd", lambda e, d=dst, s=src: e.dma_start(out=d, in_=s),
                        waits, inc=q)

        def MM(out, lhsT, rhs, start, stop, waits=(), dr=False):
            return P.op("tensor",
                        lambda e, o=out, l=lhsT, r=rhs, a=start, b=stop:
                        e.matmul(o, l, r, start=a, stop=b, skip_group_check=True),
                        waits, drain=dr)

        def DR(waits=()):
            return P.op("tensor", lambda e: e.nop(), waits, drain=True)

        def TR(out, in_, waits=()):
            return P.op("tensor", lambda e, o=out, i=in_: e.transpose(o, i, id_s[:]), waits)

        def V(fn, waits=()):
            return P.op("vector", fn, waits)

        def SC(fn, waits=()):
            return P.op("scalar", fn, waits)

        def TNOP(waits=()):
            return P.op("tensor", lambda e: e.nop(), waits)

        def barrier():
            vals = [(s, s.n) for s in ALL_SEMS]
            for name in ("sync", "gpsimd"):
                P.op(name, lambda e: e.nop(), vals, inc=MI if name == "sync" else STY)
            for name in ("tensor", "vector", "scalar"):
                P.op(name, lambda e: e.nop(), vals)

        r_sq = rowp[0:1, 0:4096]
        r_kmax = rowp[64:65, 0:512]
        r_qmax = rowp[64:65, 512:1024]
        r_sums = rowp[64:65, 1024:1536]
        r_tmp = rowp[64:65, 1536:2048]
        r_kq = rowp[64:65, 2048:2052]
        r_scale = mrow[0:1, 0:4096]
        r_rec = mrow[0:1, 4096:4608]
        negc = mrow[0:1, 4608:4609]
        negc_col = lnb[:, 8:9]
        bt_s = big1[0:32, 8192:12288]
        r_mag = big1[0:1, 12288:16384]

        cos_v = big2[:, 10240:12288]
        sin_v = big2[:, 12288:14336]
        mask_v = big2[:, 14336:16384]
        LD(MI, id_s[:], ident)
        LD(MI, on1_s[:], onesr)
        LD(MI, onc_s[:], onesc)
        LD(MI, at_s[:].rearrange("p (c r) -> p c r", c=NIC),
           at.rearrange("(c p) r -> p c r", p=128))
        LD(MI, bt_s, bt)
        LD(MI, r_mag, magr)
        LD(MI, cos_v, cosf)
        LD(MI, sin_v, sinf)
        ld_prolog = LD(MI, mask_v, maskc)

        pcpan = [big2[:, 0:4096], big2[:, 4096:8192]]
        wsq = big1[:, 0:4096]
        pan_free = [[], []]
        wsq_free = []
        pan_ready = [0, 0]
        for ic in range(NIC):
            par = ic % 2
            pan_ready[par] = LD(WP[par], pcpan[par], wt[ic * 128:(ic + 1) * 128, :],
                                waits=pan_free[par])
            wsq_ready = SC(lambda e, i=pcpan[par]: e.activation(wsq, i, AF.Square),
                           waits=[(WP[par], pan_ready[par])] + wsq_free)
            for j in range(8):
                ps = psA if j < 4 else psB
                c0 = (j % 4) * 512
                MM(ps[0:32, c0:c0 + 512], at_s[:, ic * R:(ic + 1) * R],
                   pcpan[par][:, j * 512:(j + 1) * 512],
                   start=(ic == 0), stop=(ic == NIC - 1),
                   waits=[(WP[par], pan_ready[par]), (MI, ld_prolog)])
                MM(ps[32:33, c0:c0 + 512], onc_s[:], wsq[:, j * 512:(j + 1) * 512],
                   start=(ic == 0), stop=(ic == NIC - 1),
                   waits=[(SS, wsq_ready)])
            pan_free[par] = [(SS, wsq_ready), (TT, TT.n)]
            wsq_free = [(TT, TT.n)]
        tt_pc = DR()
        ct_sb = big1[0:32, 4096:8192]
        V(lambda e: e.tensor_copy(out=big1[0:32, 4096:6144], in_=psA[0:32, 0:2048]),
          waits=[(TT, tt_pc)])
        V(lambda e: e.tensor_copy(out=big1[0:32, 6144:8192], in_=psB[0:32, 0:2048]))
        V(lambda e: e.tensor_copy(out=rowp[0:1, 0:2048], in_=psA[32:33, 0:2048]))
        vc_pc = V(lambda e: e.tensor_copy(out=rowp[0:1, 2048:4096], in_=psB[32:33, 0:2048]))
        for ic in range(NIC):
            MM(psA[0:32, 0:32], at_s[:, ic * R:(ic + 1) * R], at_s[:, ic * R:(ic + 1) * R],
               start=(ic == 0), stop=(ic == NIC - 1), waits=[(VV, vc_pc)])
        tt_g = DR()
        g_sb = lnb[0:32, 16:48]
        vg = V(lambda e: e.tensor_copy(out=g_sb, in_=psA[0:32, 0:32]), waits=[(TT, tt_g)])
        bgt_sb = big2[0:32, 0:4096]
        for j in range(8):
            ps = psA if j < 4 else psB
            c0 = (j % 4) * 512
            MM(ps[0:32, c0:c0 + 512], g_sb, bt_s[0:32, j * 512:(j + 1) * 512],
               start=True, stop=True, waits=[(VV, vg)])
        tt_bg = DR()
        V(lambda e: e.tensor_copy(out=big2[0:32, 0:2048], in_=psA[0:32, 0:2048]),
          waits=[(TT, tt_bg)])
        vb2 = V(lambda e: e.tensor_copy(out=big2[0:32, 2048:4096], in_=psB[0:32, 0:2048]))
        t32 = big2[0:32, 4096:8192]
        V(lambda e: e.scalar_tensor_tensor(out=t32, in0=ct_sb, scalar=2.0, in1=bgt_sb,
                                           op0=ALU.mult, op1=ALU.add), waits=[(VV, vb2)])
        vt32 = V(lambda e: e.tensor_mul(out=t32, in0=t32, in1=bt_s))
        for j in range(8):
            ps = psA if j < 4 else psB
            c0 = (j % 4) * 512
            MM(ps[0:1, c0:c0 + 512], onc_s[0:32, 0:1], t32[:, j * 512:(j + 1) * 512],
               start=True, stop=True, waits=[(VV, vt32)])
        tt_bcb = DR()
        V(lambda e: e.tensor_add(out=r_sq[:, 0:2048], in0=psA[0:1, 0:2048],
                                 in1=r_sq[:, 0:2048]), waits=[(TT, tt_bcb)])
        vbcb = V(lambda e: e.tensor_add(out=r_sq[:, 2048:4096], in0=psB[0:1, 0:2048],
                                        in1=r_sq[:, 2048:4096]))
        sc1 = SC(lambda e: e.activation(r_sq, r_sq, AF.Sqrt), waits=[(VV, vbcb)])
        vrec = V(lambda e: e.reciprocal(out=r_sq, in_=r_sq), waits=[(SS, sc1)])
        vscale = V(lambda e: e.tensor_mul(out=r_scale, in0=r_sq, in1=r_mag),
                   waits=[(VV, vrec)])

        barrier()

        xt = [big2[:, 0:4096], big2[:, 4096:8192]]
        xt_free = [[], []]
        xt_ready = [0, 0]
        s2a = lnb[:, 0:1]; s2b = lnb[:, 1:2]; s2 = lnb[:, 2:3]
        negm = lnb[:, 3:4]; m2e = lnb[:, 4:5]; rstd = lnb[:, 5:6]; negmr = lnb[:, 6:7]
        msum = lnb[:, 7:8]
        b1v = big1[:].rearrange("p (c n) -> p c n", c=NIC)
        last_scx = SS.n
        for sidx in range(NSTR):
            stx_prev = STX.n
            for ti in range(4):
                t = sidx * 4 + ti
                par = t % 2
                xt_ready[par] = LD(XT[par], xt[par], x[t * 128:(t + 1) * 128, :],
                                   waits=xt_free[par])
                wld = [(XT[par], xt_ready[par]), (SS, last_scx)]
                v1 = V(lambda e, p=par: e.reduce_sum(out=msum, in_=xt[p], axis=AX),
                       waits=wld)
                sqa = SC(lambda e, p=par: e.activation(psA[:, 0:2048], xt[p][:, 0:2048],
                                                       AF.Square, accum_out=s2a),
                         waits=[(XT[par], xt_ready[par]), (VV, VV.n), (TT, TT.n)])
                sqb = SC(lambda e, p=par: e.activation(psB[:, 0:2048], xt[p][:, 2048:4096],
                                                       AF.Square, accum_out=s2b))
                SC(lambda e: e.mul(negm, msum, -1.0 / D), waits=[(VV, v1)])
                SC(lambda e: e.activation(m2e, negm, AF.Square))
                ssv = SS.n
                v2 = V(lambda e: e.tensor_add(out=s2, in0=s2a, in1=s2b), waits=[(SS, ssv)])
                v3 = V(lambda e: e.tensor_scalar(out=m2e, in0=m2e, scalar1=-1.0,
                                                 scalar2=EPS, op0=ALU.mult, op1=ALU.add))
                sr = SC(lambda e: e.activation(rstd, s2, AF.Sqrt, bias=m2e, scale=1.0 / D),
                        waits=[(VV, v3)])
                v4a = V(lambda e: e.reciprocal(out=rstd, in_=rstd), waits=[(SS, sr)])
                v4 = V(lambda e: e.tensor_mul(out=negmr, in0=negm, in1=rstd))
                scx = SC(lambda e, p=par: e.activation(xt[p], xt[p], AF.Identity,
                                                       bias=negmr, scale=rstd),
                         waits=[(VV, v4)])
                last_scx = scx
                for g in range(2):
                    ps = psA if g == 0 else psB
                    for k in range(16):
                        icn = g * 16 + k
                        TR(ps[:, k * 128:(k + 1) * 128],
                           xt[par][:, icn * 128:(icn + 1) * 128],
                           waits=[(SS, scx), (VV, VV.n), (STX, stx_prev)])
                    ttv = DR()
                    psv = ps[:].rearrange("p (c n) -> p c n", c=16)
                    for q in range(4):
                        V(lambda e, i0=g * 16 + q * 4, pv=psv, qq=q, tti=ti:
                          e.tensor_copy(out=b1v[:, i0:i0 + 4, tti * 128:(tti + 1) * 128],
                                        in_=pv[:, qq * 4:qq * 4 + 4, :]),
                          waits=[(TT, ttv)])
                xt_free[par] = [(VV, VV.n)]
            vdone = VV.n
            for icn in range(NIC):
                ST(STX, xnf[icn * 128:(icn + 1) * 128, sidx * 512:(sidx + 1) * 512],
                   big1[:, icn * 512:icn * 512 + 512], waits=[(VV, vdone)])

        barrier()

        kb = [big2[:, 0:512], big2[:, 512:1024], big2[:, 1024:1536], big2[:, 1536:2048]]
        wpan = [big2[:, 2048:3072], big2[:, 3072:4096]]
        wpan_free = [[], []]
        wpan_ready = [0, 0]

        def bank(j):
            return (psA if j < 4 else psB)[:, (j % 4) * 512:(j % 4) * 512 + 512]

        state = {"evac": VV.n, "fk": True, "fq": True}
        pgates = [dict(st=0, nm=0, sq=0), dict(st=0, nm=0, sq=0)]

        def proj_pass(sidx, wsrc, col0, dst, drow0, rope, norms, maxrow, firstkey, lds):
            for ic in range(NIC):
                par = ic % 2
                wpan_ready[par] = LD(WP[par], wpan[par],
                                     wsrc[ic * 128:(ic + 1) * 128, col0:col0 + 1024],
                                     waits=wpan_free[par])
                w0 = [(WP[par], wpan_ready[par]), (XNS, lds)]
                if ic == 0:
                    w0.append((VV, state["evac"]))
                for j in range(8):
                    MM(bank(j), wpan[par][:, j * 128:(j + 1) * 128],
                       big1[:, ic * 512:(ic + 1) * 512],
                       start=(ic == 0), stop=(ic == NIC - 1), waits=w0)
                wpan_free[par] = [(TT, TT.n)]
            ttmm = DR()
            gates = pgates
            for j in range(8):
                p = j % 2
                ba, bb = kb[2 * p], kb[2 * p + 1]
                w0 = [(TT, ttmm)]
                if gates[p]["st"]:
                    w0.append((STK[p], gates[p]["st"]))
                if gates[p]["nm"]:
                    w0.append((TT, gates[p]["nm"]))
                if gates[p]["sq"]:
                    w0.append((SS, gates[p]["sq"]))
                if rope:
                    cs_ = cos_v[:, sidx * 512:(sidx + 1) * 512]
                    sn_ = sin_v[:, sidx * 512:(sidx + 1) * 512]
                    V(lambda e, jj=j, b=ba, c=cs_: e.tensor_mul(out=b, in0=bank(jj), in1=c),
                      waits=w0)
                    V(lambda e, jj=j, b=bb: e.tensor_copy(out=b[0:64, :],
                                                          in_=bank(jj)[64:128, :]), waits=w0)
                    V(lambda e, jj=j, b=bb: e.tensor_copy(out=b[64:128, :],
                                                          in_=bank(jj)[0:64, :]))
                    V(lambda e, b=bb, sn=sn_: e.tensor_mul(out=b, in0=b, in1=sn))
                    vr = V(lambda e, a=ba, b=bb: e.tensor_add(out=a, in0=a, in1=b))
                else:
                    vr = V(lambda e, jj=j, b=ba: e.tensor_copy(out=b, in_=bank(jj)),
                           waits=w0)
                gates[p]["st"] = ST(STK[p], dst[drow0 + j * 128:drow0 + (j + 1) * 128,
                                                sidx * 512:(sidx + 1) * 512], ba,
                                    waits=[(VV, vr)])
                if norms:
                    gates[p]["sq"] = SC(lambda e, a=ba, b=bb: e.activation(b, a, AF.Square),
                                        waits=[(VV, vr)])
                    nm = MM(bank(j)[0:1, 0:512], onc_s[:], bb, start=True, stop=True,
                            waits=[(SS, gates[p]["sq"])], dr=True)
                    gates[p]["nm"] = nm
                    vt_ = V(lambda e, jj=j: e.tensor_copy(out=r_tmp,
                                                          in_=bank(jj)[0:1, 0:512]),
                            waits=[(TT, nm)])
                    if state[firstkey]:
                        V(lambda e, r=maxrow: e.tensor_copy(out=r, in_=r_tmp))
                        state[firstkey] = False
                    else:
                        V(lambda e, r=maxrow: e.tensor_max(out=r, in0=r, in1=r_tmp))
            state["evac"] = VV.n

        for sidx in range(NSTR):
            lds = 0
            for icn in range(NIC):
                lds = LD(XNS, big1[:, icn * 512:icn * 512 + 512],
                         xnf[icn * 128:(icn + 1) * 128, sidx * 512:(sidx + 1) * 512],
                         waits=[(VV, state["evac"]), (TT, TT.n)])
            proj_pass(sidx, wk, 0, kf, 0, rope=True, norms=True,
                      maxrow=r_kmax, firstkey="fk", lds=lds)
            proj_pass(sidx, wv, 0, vf, 0, rope=False, norms=False,
                      maxrow=None, firstkey="fk", lds=lds)
            for dq in range(4):
                proj_pass(sidx, wq, dq * 1024, qf, dq * 1024, rope=True, norms=True,
                          maxrow=r_qmax, firstkey="fq", lds=lds)
        vkm = V(lambda e: e.reduce_max(out=r_kq[:, 0:1], in_=r_kmax, axis=AX),
                waits=[(VV, state["evac"])])
        vqm = V(lambda e: e.reduce_max(out=r_kq[:, 1:2], in_=r_qmax, axis=AX))
        sckq = SC(lambda e: e.activation(r_kq[:, 0:2], r_kq[:, 0:2], AF.Sqrt),
                  waits=[(VV, vqm)])
        vmu = V(lambda e: e.tensor_mul(out=negc, in0=r_kq[:, 0:1], in1=r_kq[:, 1:2]),
                waits=[(SS, sckq)])
        scnc = SC(lambda e: e.mul(negc, negc, -ISQHD), waits=[(VV, vmu)])
        bm = MM(psA[:, 0:1], on1_s[:], negc, start=True, stop=True,
                waits=[(SS, scnc), (VV, VV.n)], dr=True)
        V(lambda e: e.tensor_copy(out=negc_col, in_=psA[:, 0:1]), waits=[(TT, bm)])

        barrier()

        Ks = big2[:, 0:2048]
        Vs = big2[:, 2048:4096]
        vts = big2[:, 4096:6144]
        Qs = big2[:, 6144:6656]
        PTb = [big2[:, 6656:7168], big2[:, 7168:7680]]
        Osb = big2[:, 7680:8192]
        rbc = big2[:, 8192:8704]
        psc = [psA[:, 0:512], psA[:, 512:1024]]
        ps_o = psA[:, 1024:1536]
        ps_bc = psA[:, 1536:2048]
        ps_sum = psB[0:1, 0:512]
        ps_vt = psB[:, 512:1024]

        kv_gate = [(TT, TT.n)]
        for sidx in range(NSTR):
            KC = 4 * (sidx + 1)
            klen = KC * 128
            for kv in range(NKV):
                lk = LD(KVS, Ks[:, 0:klen], kf[kv * 128:(kv + 1) * 128, 0:klen],
                        waits=kv_gate)
                lv = LD(KVS, Vs[:, 0:klen], vf[kv * 128:(kv + 1) * 128, 0:klen],
                        waits=kv_gate)
                for g in range((KC + 3) // 4):
                    n = min(4, KC - g * 4)
                    for k in range(n):
                        TR(ps_vt[:, k * 128:(k + 1) * 128],
                           Vs[:, (g * 4 + k) * 128:(g * 4 + k + 1) * 128],
                           waits=[(KVS, lv), (VV, VV.n)])
                    tv = DR()
                    V(lambda e, gg=g, nn=n: e.tensor_copy(
                        out=vts[:, gg * 512:gg * 512 + nn * 128], in_=ps_vt[:, 0:nn * 128]),
                      waits=[(TT, tv)])
                vt_done = VV.n
                for qh in range(4):
                    qhead = kv * 4 + qh
                    lq = LD(KVS, Qs, qf[qhead * 128:(qhead + 1) * 128,
                                        sidx * 512:(sidx + 1) * 512],
                            waits=[(TT, TT.n)])
                    exp_done = SS.n
                    first_w = [(VV, VV.n), (SS, SS.n)]
                    for kc in range(KC):
                        par = kc % 2
                        ms = MM(psc[par], Ks[:, kc * 128:(kc + 1) * 128], Qs,
                                start=True, stop=True,
                                waits=[(KVS, lq), (SS, exp_done)], dr=True)
                        if kc >= KC - 4:
                            rel = kc - (KC - 4)
                            mv = V(lambda e, p=par, r=rel: e.tensor_add(
                                out=psc[p], in0=psc[p],
                                in1=mask_v[:, r * 512:(r + 1) * 512]),
                                waits=[(TT, ms)])
                            wexp = [(VV, mv)]
                        else:
                            wexp = [(TT, ms)]
                        exp_done = SC(lambda e, p=par: e.activation(
                            PTb[p], psc[p], AF.Exp, bias=negc_col, scale=ISQHD),
                            waits=wexp + [(TT, TT.n)])
                        MM(ps_sum, onc_s[:], PTb[par], start=(kc == 0), stop=(kc == KC - 1),
                           waits=[(SS, exp_done), (VV, vt_done)] + (first_w if kc == 0 else []))
                        MM(ps_o, vts[:, kc * 128:(kc + 1) * 128], PTb[par],
                           start=(kc == 0), stop=(kc == KC - 1))
                    tmm = DR()
                    vs_ = V(lambda e: e.tensor_copy(out=r_sums, in_=ps_sum),
                            waits=[(TT, tmm)])
                    vr_ = V(lambda e: e.reciprocal(out=r_rec, in_=r_sums))
                    bmv = MM(ps_bc, on1_s[:], r_rec, start=True, stop=True,
                             waits=[(VV, vr_), (SS, SS.n)], dr=True)
                    se_ = SC(lambda e: e.copy(rbc, ps_bc), waits=[(TT, bmv)])
                    vo_ = V(lambda e: e.tensor_mul(out=Osb, in0=ps_o, in1=rbc),
                            waits=[(SS, se_), (STO, STO.n)])
                    ST(STO, of[qhead * 128:(qhead + 1) * 128, sidx * 512:(sidx + 1) * 512],
                       Osb, waits=[(VV, vo_)])
                kv_gate = [(TT, TT.n)]

        barrier()

        scale_bc = big1[:, 12288:16384]
        for j in range(8):
            ps = psA if j < 4 else psB
            c0 = (j % 4) * 512
            MM(ps[:, c0:c0 + 512], on1_s[:], r_scale[0:1, j * 512:(j + 1) * 512],
               start=True, stop=True, waits=[(VV, vscale)])
        tsc = DR()
        V(lambda e: e.tensor_copy(out=big1[:, 12288:14336], in_=psA[:, 0:2048]),
          waits=[(TT, tsc)])
        vsc2 = V(lambda e: e.tensor_copy(out=big1[:, 14336:16384], in_=psB[:, 0:2048]))
        ldbt = LD(MI, bt_s, bt)
        TNOP(waits=[(VV, vsc2), (MI, ldbt)])

        wpan = [big1[:, 0:1024], big1[:, 1024:2048]]
        gt1 = big1[:, 4096:4608]
        gt2 = big1[:, 4608:5120]
        t_sb = big1[:, 5120:5248]
        tt_sb = big1[0:32, 5248:5760]
        strm = [big1[:, 5760:6272], big1[:, 6272:6784]]
        strm_free = [[], []]
        strm_ready = [0, 0]

        for sidx in range(NSTR):
            sty_strip = STY.n
            for ic in range(NIC):
                par = ic % 2
                strm_ready[par] = LD(SA[par], strm[par],
                                     xnf[ic * 128:(ic + 1) * 128,
                                         sidx * 512:(sidx + 1) * 512],
                                     waits=strm_free[par])
                for sub in range(4):
                    MM(psA[:, 1024 + sub * 32:1024 + sub * 32 + 32],
                       strm[par][:, sub * 128:(sub + 1) * 128],
                       at_s[:, ic * R:(ic + 1) * R],
                       start=(ic == 0), stop=(ic == NIC - 1),
                       waits=[(SA[par], strm_ready[par])])
                strm_free[par] = [(TT, TT.n)]
            dtp = DR()
            vte = V(lambda e: e.tensor_copy(out=t_sb, in_=psA[:, 1024:1152]),
                    waits=[(TT, dtp)])
            for sub in range(4):
                TR(psA[0:32, 1280 + sub * 128:1280 + (sub + 1) * 128],
                   t_sb[:, sub * 32:(sub + 1) * 32], waits=[(VV, vte)])
            dtr = DR()
            vtt = V(lambda e: e.tensor_copy(out=tt_sb, in_=psA[0:32, 1280:1792]),
                    waits=[(TT, dtr)])
            TNOP(waits=[(VV, vtt)])

            for dq in range(4):
                for ic in range(NIC):
                    par = ic % 2
                    strm_ready[par] = LD(SA[par], strm[par],
                                         of[ic * 128:(ic + 1) * 128,
                                            sidx * 512:(sidx + 1) * 512],
                                         waits=strm_free[par])
                    wpan_ready[par] = LD(WP[par], wpan[par],
                                         wo[ic * 128:(ic + 1) * 128,
                                            dq * 1024:(dq + 1) * 1024],
                                         waits=wpan_free[par])
                    wmm = [(SA[par], strm_ready[par]), (WP[par], wpan_ready[par])]
                    if ic == 0:
                        wmm.append((VV, VV.n))
                    for sub in range(4):
                        for h in range(2):
                            MM(bank(sub * 2 + h),
                               strm[par][:, sub * 128:(sub + 1) * 128],
                               wpan[par][:, h * 512:(h + 1) * 512],
                               start=(ic == 0), stop=(ic == NIC - 1), waits=wmm)
                    gate = [(TT, TT.n)]
                    strm_free[par] = gate
                    wpan_free[par] = gate
                ttA = DR()
                for sub in range(4):
                    for h in range(2):
                        c0 = sub * 4096 + dq * 1024 + h * 512
                        wA = [(TT, ttA)]
                        if dq == 0:
                            wA.append((STY, sty_strip))
                        V(lambda e, cc=c0, bk=bank(sub * 2 + h):
                          e.tensor_copy(out=big2[:, cc:cc + 512], in_=bk), waits=wA)
                vA = VV.n
                TNOP(waits=[(VV, vA)])
                for ic in range(NIC):
                    par = ic % 2
                    strm_ready[par] = LD(SA[par], strm[par],
                                         xnf[ic * 128:(ic + 1) * 128,
                                             sidx * 512:(sidx + 1) * 512],
                                         waits=strm_free[par])
                    wpan_ready[par] = LD(WP[par], wpan[par],
                                         wt[ic * 128:(ic + 1) * 128,
                                            dq * 1024:(dq + 1) * 1024],
                                         waits=wpan_free[par])
                    wmm = [(SA[par], strm_ready[par]), (WP[par], wpan_ready[par])]
                    for sub in range(4):
                        for h in range(2):
                            MM(bank(sub * 2 + h),
                               strm[par][:, sub * 128:(sub + 1) * 128],
                               wpan[par][:, h * 512:(h + 1) * 512],
                               start=(ic == 0), stop=False, waits=wmm)
                    gate = [(TT, TT.n)]
                    strm_free[par] = gate
                    wpan_free[par] = gate
                for sub in range(4):
                    for h in range(2):
                        MM(bank(sub * 2 + h), tt_sb[:, sub * 128:(sub + 1) * 128],
                           bt_s[0:32, dq * 1024 + h * 512:dq * 1024 + h * 512 + 512],
                           start=False, stop=True)
                ttB = DR()
                for sub in range(4):
                    for h in range(2):
                        c0 = sub * 4096 + dq * 1024 + h * 512
                        sc_sl = scale_bc[:, dq * 1024 + h * 512:dq * 1024 + h * 512 + 512]
                        vg1 = V(lambda e, bk=bank(sub * 2 + h), sl=sc_sl:
                                e.tensor_mul(out=gt1, in0=bk, in1=sl),
                                waits=[(TT, ttB), (SS, SS.n)])
                        sg1 = SC(lambda e: e.activation(gt2, gt1, AF.Silu),
                                 waits=[(VV, vg1)])
                        V(lambda e, cc=c0: e.tensor_mul(out=big2[:, cc:cc + 512],
                                                        in0=big2[:, cc:cc + 512], in1=gt2),
                          waits=[(SS, sg1)])
                vB = VV.n
                TNOP(waits=[(VV, vB)])
            for sub in range(4):
                t0 = sidx * 512 + sub * 128
                va = 0
                for c in range(8):
                    xb = gt1 if c % 2 == 0 else gt2
                    lx = LD(MI, xb, x[t0:t0 + 128, c * 512:(c + 1) * 512],
                            waits=[(VV, VV.n)])
                    va = V(lambda e, sb_=sub, cc=c, b=xb: e.tensor_add(
                        out=big2[:, sb_ * 4096 + cc * 512:sb_ * 4096 + cc * 512 + 512],
                        in0=big2[:, sb_ * 4096 + cc * 512:sb_ * 4096 + cc * 512 + 512],
                        in1=b), waits=[(MI, lx)])
                ST(STY, y[t0:t0 + 128, :], big2[:, sub * 4096:(sub + 1) * 4096],
                   waits=[(VV, va)])

        P.op("gpsimd", lambda e: e.nop(),
             waits=[(STY, STY.n), (STX, STX.n), (STK[0], STK[0].n),
                    (STK[1], STK[1].n), (STO, STO.n)], inc=STY)

        P.finalize()

        @block.sync
        def _(sync):
            P.emit_engine("sync", sync)

        @block.gpsimd
        def _(gpsimd):
            P.emit_engine("gpsimd", gpsimd)

        @block.tensor
        def _(tensor):
            P.emit_engine("tensor", tensor)

        @block.vector
        def _(vector):
            P.emit_engine("vector", vector)

        @block.scalar
        def _(scalar):
            P.emit_engine("scalar", scalar)
    return nc


def _host_consts():
    inv = 1.0 / (10000.0 ** (np.arange(0, HD, 2, dtype=np.float64)[: HD // 2] / HD))
    freqs = np.arange(S, dtype=np.float64)[None, :] * inv[:, None]
    cos64 = np.cos(freqs)
    sin64 = np.sin(freqs)
    cosf = np.concatenate([cos64, cos64], axis=0).astype(np.float32)
    sinf = np.concatenate([-sin64, sin64], axis=0).astype(np.float32)
    mask = np.zeros((128, 4 * STRIP), dtype=np.float32)
    p = np.arange(128)[:, None]
    f = np.arange(STRIP)[None, :]
    for rel in range(4):
        mask[:, rel * STRIP:(rel + 1) * STRIP] = np.where(rel * 128 + p <= f, 0.0, NEG)
    ident = np.eye(128, dtype=np.float32)
    onesr = np.ones((1, 128), dtype=np.float32)
    onesc = np.ones((128, 1), dtype=np.float32)
    return cosf, sinf, mask, ident, onesr, onesc


def kernel(L0_attn_x, L0_attn_Wq, L0_attn_Wk, L0_attn_Wv, L0_attn_Wo,
           L0_ffn_W, L0_ffn_lora_A, L0_ffn_lora_B, L0_ffn_magnitude):
    x = np.ascontiguousarray(np.asarray(L0_attn_x, dtype=np.float32))
    wq_ = np.ascontiguousarray(np.asarray(L0_attn_Wq, dtype=np.float32))
    wk_ = np.ascontiguousarray(np.asarray(L0_attn_Wk, dtype=np.float32))
    wv_ = np.ascontiguousarray(np.asarray(L0_attn_Wv, dtype=np.float32))
    wo_ = np.ascontiguousarray(np.asarray(L0_attn_Wo, dtype=np.float32))
    wt_ = np.ascontiguousarray(np.asarray(L0_ffn_W, dtype=np.float32).T)
    at_ = np.ascontiguousarray(np.asarray(L0_ffn_lora_A, dtype=np.float32).T)
    bt_ = np.ascontiguousarray(np.asarray(L0_ffn_lora_B, dtype=np.float32).T)
    magr_ = np.ascontiguousarray(np.asarray(L0_ffn_magnitude, dtype=np.float32)[None, :])
    cosf, sinf, mask, ident, onesr, onesc = _host_consts()

    nc = build_nc()
    shared = dict(wq=wq_, wk=wk_, wv=wv_, wo=wo_, wt=wt_, at=at_, bt=bt_, magr=magr_,
                  cosf=cosf, sinf=sinf, maskc=mask, ident=ident, onesr=onesr,
                  onesc=onesc)
    in_maps = [dict(x=np.ascontiguousarray(x[b]), **shared) for b in range(8)]
    res = run_bass_kernel_spmd(nc, in_maps, core_ids=list(range(8)))
    out = np.stack([res.results[b]["y"] for b in range(8)], axis=0)
    return out.astype(np.float32)
